# version 01
# baseline (speedup 1.0000x reference)
import numpy as np
import ml_dtypes
import concourse.bass as bass
import concourse.mybir as mybir
from concourse.alu_op_type import AluOpType
from concourse.bass_utils import run_bass_kernel_spmd
from contextlib import ExitStack

F32 = mybir.dt.float32
F16 = mybir.dt.float16
BF16 = mybir.dt.bfloat16
I32 = mybir.dt.int32
ACT = mybir.ActivationFunctionType
AL = AluOpType

DM = 2048
NT = DM // 128
IF = 6464
NI = 51
NST = 16
DTR = 128
T_OWN = 1024
HALO = 3
T = T_OWN + HALO
TC = 512
ROWW = 352
C23 = float(1.5 * 2**23)
D52 = float(1.5 * 2**52)
T24 = float(1.5 * 2**24)
EPS = 1e-5

def fchunks(width, maxw=512):
    n = (width + maxw - 1) // maxw
    base = width // n
    rem = width - base * n
    out = []
    off = 0
    for i in range(n):
        w = base + (1 if i < rem else 0)
        out.append((off, w))
        off += w
    return out

def itile_sizes():
    return [(k * 128, 128) for k in range(50)] + [(6400, 64)]


class Prog:
    def __init__(self, nc):
        self.nc = nc
        self.sems = {}
        self.cnt = {}

    def sem(self, name, ctx):
        s = ctx.enter_context(self.nc.semaphore(name))
        self.sems[name] = s
        self.cnt[name] = 0
        return s

    def inc(self, name, by=1):
        self.cnt[name] += by
        return self.sems[name], by

    def tgt(self, name):
        return self.sems[name], self.cnt[name]


def host_prep_consts():
    consts = {}
    consts["c_meanones"] = np.full((128, 1), 1.0 / DM, np.float32)
    consts["c_ones128"] = np.full((128, 1), 1.0, np.float32)
    consts["c_ones1x128"] = np.full((1, 128), 1.0, np.float32)
    for k in range(7):
        s = 64 >> k
        m = np.zeros((128, s), np.float32)
        for j in range(s):
            m[j + s, j] = 1.0
        consts[f"c_shift{k}"] = m
    bc = np.zeros((16, 16 * 128), np.float16)
    for n in range(16):
        bc[n, n * 128:(n + 1) * 128] = 1.0
    consts["c_bcmask"] = bc
    return consts


def shard_inputs(ins):
    x = np.asarray(ins["L0_attn_x"])
    cores = []
    for c in range(8):
        b, chunk = c // 4, c % 4
        t0 = chunk * T_OWN
        xs = x[b, max(t0 - HALO, 0):t0 + T_OWN, :]
        if t0 == 0:
            xs = np.concatenate([np.zeros((HALO, DM), np.float32), xs], 0)
        cores.append(np.ascontiguousarray(xs.T))
    return cores


def weight_prep(ins):
    w = {}
    w["wgT"] = np.ascontiguousarray(np.asarray(ins["L0_ffn_w_gate"]).T)
    w["wuT"] = np.ascontiguousarray(np.asarray(ins["L0_ffn_w_up"]).T)
    w["wdT"] = np.ascontiguousarray(np.asarray(ins["L0_ffn_w_down"]).T)
    w["winT"] = np.ascontiguousarray(np.asarray(ins["L0_attn_W_in"]))
    w["cwT"] = np.ascontiguousarray(np.asarray(ins["L0_attn_conv_w"]).transpose(2, 1, 0))
    w["woT"] = np.ascontiguousarray(np.asarray(ins["L0_attn_W_out"]))
    w["wxT"] = np.ascontiguousarray(np.asarray(ins["L0_attn_W_x"]))
    w["wdtT"] = np.ascontiguousarray(np.asarray(ins["L0_attn_W_dt"]))
    w["A_log"] = np.ascontiguousarray(np.asarray(ins["L0_attn_A_log"]))
    w["conv_b"] = np.ascontiguousarray(np.asarray(ins["L0_attn_conv_b"]).reshape(NT, 128, 1))
    w["b_dt"] = np.ascontiguousarray(np.asarray(ins["L0_attn_b_dt"]).reshape(NT, 128, 1))
    w["Dvec"] = np.ascontiguousarray(np.asarray(ins["L0_attn_D"]).reshape(NT, 128, 1))
    return w



def build_l1(debug=None, stop_after=None):
    debug = debug or set()
    nc = bass.Bass("TRN2", target_bir_lowering=False, detect_race_conditions=False)
    p = Prog(nc)

    ap = {}
    def din(name, shape, dt=F32):
        ap[name] = nc.dram_tensor(name, shape, dt, kind="ExternalInput").ap()
    def dout(name, shape, dt=F32):
        ap[name] = nc.dram_tensor(name, shape, dt, kind="ExternalOutput").ap()

    din("xT", [DM, T])
    din("wgT", [DM, IF]); din("wuT", [DM, IF]); din("wdT", [IF, DM])
    din("winT", [DM, 2 * DM]); din("cwT", [4, DM, DM])
    din("wxT", [DM, DTR + 2 * NST]); din("wdtT", [DTR, DM])
    din("A_log", [DM, NST])
    din("conv_b", [NT, 128, 1]); din("b_dt", [NT, 128, 1]); din("Dvec", [NT, 128, 1])
    din("c_meanones", [128, 1]); din("c_ones128", [128, 1]); din("c_ones1x128", [1, 128])
    for k in range(7):
        din(f"c_shift{k}", [128, 64 >> k])
    din("c_bcmask", [16, 16 * 128], F16)

    wq_g = nc.dram_tensor("wq_g", [DM, IF], F16, kind="ExternalOutput").ap()
    wq_u = nc.dram_tensor("wq_u", [DM, IF], F16, kind="ExternalOutput").ap()
    wq_d = nc.dram_tensor("wq_d", [IF, DM], F16, kind="ExternalOutput").ap()

    dout("o_x1", [DM, T])
    dout("o_u", [DM, T_OWN], F16)
    dout("o_sr", [DM, T_OWN], F16)
    dout("o_delta", [DM, T_OWN])
    dout("o_B", [16, T_OWN], F16)
    dout("o_C", [16, T_OWN], F16)
    dout("o_F", [128, NT * NST])
    dout("o_Tdec", [128, NT * NST])
    dout("o_mw", [1, 4])
    for nm in debug:
        if nm == "q1":
            dout("dbg_q1", [DM, T], F16)
        if nm == "stats":
            dout("dbg_stats", [8, T])
        if nm == "wq":
            dout("dbg_wq", [128, IF], F16)
        if nm == "qp":
            dout("dbg_qp", [128, T], F16)
        if nm == "dsg":
            dout("dbg_dsg", [1, T])
    dout("dbg_rows", [2, T])

    with ExitStack() as ctx:
        sb = lambda name, shape, dt=F32: ctx.enter_context(nc.sbuf_tensor(name, shape, dt))
        ps = lambda name, shape: ctx.enter_context(nc.psum_tensor(name, shape, F32))
        for s in ("qs", "qa", "qg", "tv", "vv", "sv", "gv"):
            p.sem(s, ctx)
        p.gap = sb("pgap", [128, 1])

        x = [sb(f"x{g}", [128, T]) for g in range(NT)]
        cmag = sb("cmag", [128, 1])
        onecol = sb("onecol", [128, 1])
        gap0 = sb("gap0", [128, 1])
        epscol = sb("epscol", [128, 1])
        A_t = [sb(f"A{g}", [128, NST]) for g in range(NT)]
        convb_t = [sb(f"cb{g}", [128, 1]) for g in range(NT)]
        bdt_t = [sb(f"bdt{g}", [128, 1]) for g in range(NT)]
        D_t = [sb(f"Dv{g}", [128, 1]) for g in range(NT)]
        meanones = sb("meanones", [128, 1])
        ones128 = sb("ones128", [128, 1])
        ones1x128 = sb("ones1x128", [1, 128])
        shift = [sb(f"shift{k}", [128, 64 >> k]) for k in range(7)]
        bcmask = sb("bcmask", [16, 16 * 128], F16)
        mw = sb("mw", [1, 4])
        mw_pp = sb("mw_pp", [128, 4])
        rows = {k: sb(f"row_{k}", [1, ROWW]) for k in
                ("m", "msq", "den", "sig", "se", "rcp", "q", "g", "u", "d", "rs", "dd")}

        pb = [ps(f"pb{i}", [128, 512]) for i in range(6)]
        prowA = ps("prowA", [128, 512])
        prowB = ps("prowB", [128, 512])

        with nc.Block() as blk:
            @blk.sync
            def _(e):
                for g in range(NT):
                    e.dma_start(out=x[g][:], in_=ap["xT"][g * 128:(g + 1) * 128, :]).then_inc(*p.inc("qs", 16))
                e.dma_start(out=meanones[:], in_=ap["c_meanones"]).then_inc(*p.inc("qs", 16))
                e.dma_start(out=ones128[:], in_=ap["c_ones128"]).then_inc(*p.inc("qs", 16))
                e.dma_start(out=ones1x128[:], in_=ap["c_ones1x128"]).then_inc(*p.inc("qs", 16))
                for k in range(7):
                    e.dma_start(out=shift[k][:], in_=ap[f"c_shift{k}"]).then_inc(*p.inc("qs", 16))
                e.dma_start(out=bcmask[:], in_=ap["c_bcmask"]).then_inc(*p.inc("qs", 16))
                for g in range(NT):
                    e.dma_start(out=convb_t[g][:], in_=ap["conv_b"][g]).then_inc(*p.inc("qs", 16))
                    e.dma_start(out=bdt_t[g][:], in_=ap["b_dt"][g]).then_inc(*p.inc("qs", 16))
                    e.dma_start(out=D_t[g][:], in_=ap["Dvec"][g]).then_inc(*p.inc("qs", 16))
            load_done = p.cnt["qs"]

            @blk.gpsimd
            def _(e):
                for g in range(NT):
                    e.dma_start(out=A_t[g][:], in_=ap["A_log"][g * 128:(g + 1) * 128, :]).then_inc(*p.inc("qg", 16))
            alog_done = p.cnt["qg"]

            @blk.vector
            def _(e):
                e.memset(epscol[:], EPS)
                e.memset(onecol[:], 1.0)
                e.memset(gap0[:], 0.0)
                e.memset(gap0[:], 0.0)
                e.tensor_scalar(cmag[:], onecol[:], T24, T24, AL.add, AL.subtract)
                e.memset(gap0[:], 0.0)
                e.memset(gap0[:], 0.0)
                e.tensor_scalar(cmag[:], cmag[:], D52, C23, AL.mult, AL.add).then_inc(*p.inc("vv"))
            cmag_done = p.cnt["vv"]

            @blk.scalar
            def _(e):
                e.wait_ge(p.sems["qg"], alog_done)
                for g in range(NT):
                    ins = e.activation(A_t[g][:], A_t[g][:], ACT.Exp)
                ins.then_inc(*p.inc("sv"))
            aexp_done = p.cnt["sv"]

            @blk.vector
            def _(e):
                e.wait_ge(p.sems["sv"], aexp_done)
                for g in range(NT):
                    ins = e.tensor_scalar(A_t[g][:], A_t[g][:], -1.0, None, AL.mult)
                ins.then_inc(*p.inc("vv"))
            aneg_done = p.cnt["vv"]
        p.marks = dict(load_done=load_done, cmag_done=cmag_done, aneg_done=aneg_done)

        if stop_after == "p0":
            emit_final_export(nc, p, ap, x)
            return nc, ap
        emit_wquant(nc, p, ap, ctx, dict(mw=mw, mw_pp=mw_pp, cmag=cmag, meanones=meanones,
                                         ones128=ones128, ones1x128=ones1x128,
                                         prow=prowA, wq_g=wq_g, wq_u=wq_u, wq_d=wq_d))
        if "o_mw" in ap:
            with nc.Block() as blk:
                @blk.gpsimd
                def _(e):
                    e.wait_ge(p.sems["vv"], p.cnt["vv"])
                    e.dma_start(out=ap["o_mw"], in_=mw[:]).then_inc(*p.inc("qg", 16))
                    if "dbg_wq" in ap:
                        e.wait_ge(p.sems["qg"], p.cnt["qg"])
                        e.dma_start(out=ap["dbg_wq"], in_=wq_g[0:128, :]).then_inc(*p.inc("qg", 16))

        if stop_after == "wquant":
            emit_final_export(nc, p, ap, x)
            return nc, ap
        ffn_tiles = dict(x=x, cmag=cmag, meanones=meanones, ones128=ones128,
                         ones1x128=ones1x128, shift=shift, rows=rows,
                         mw=mw, mw_pp=mw_pp, pb=pb, prowA=prowA, prowB=prowB,
                         epscol=epscol, Tw=T, x_ready=("qs", load_done))
        emit_ffn(nc, p, ap, ctx, ffn_tiles, wq_g, wq_u, wq_d, add_scale=0.5,
                 debug=debug)
        if stop_after == "ffn1":
            emit_final_export(nc, p, ap, x)
            return nc, ap

        mamba_tiles = dict(ffn_tiles)
        mamba_tiles.update(A_t=A_t, convb_t=convb_t, bdt_t=bdt_t, bcmask=bcmask)
        emit_mamba_l1(nc, p, ap, ctx, mamba_tiles, stop_after=stop_after)

        emit_final_export(nc, p, ap, x)
    return nc, ap


def emit_final_export(nc, p, ap, x):
    with nc.Block() as blk:
        @blk.gpsimd
        def _(e):
            e.wait_ge(p.sems["vv"], p.cnt["vv"])
            e.wait_ge(p.sems["sv"], p.cnt["sv"])
            e.wait_ge(p.sems["tv"], p.cnt["tv"])
            for g in range(NT):
                e.dma_start(out=ap["o_x1"][g * 128:(g + 1) * 128, :], in_=x[g][:]).then_inc(*p.inc("qg", 16))
            e.wait_ge(p.sems["qg"], p.cnt["qg"])


def emit_wquant(nc, p, ap, ctx, t):
    mats = [("wgT", t["wq_g"], [DM, IF], 0), ("wuT", t["wq_u"], [DM, IF], 1),
            ("wdT", t["wq_d"], [IF, DM], 2)]
    with ExitStack() as arena:
        sb = lambda name, shape, dt=F32: arena.enter_context(nc.sbuf_tensor(name, shape, dt))
        wbuf = [sb(f"wqin{i}", [128, IF]) for i in range(2)]
        qbuf = [sb(f"wqout{i}", [128, IF], F16) for i in range(2)]
        acc = [sb(f"wqacc{i}", [128, 1]) for i in range(2)]
        red = sb("wqred", [128, 1])
        wide = sb("wqwide", [128, 256])
        gapw = sb("wqgapw", [128, 512])
        gapt = sb("wqgap", [128, 1])
        swr = sb("swr", [128, 4])
        prow = t["prow"]; mw = t["mw"]; mw_pp = t["mw_pp"]; cmag = t["cmag"]
        ones128 = t["ones128"]; ones1x128 = t["ones1x128"]

        def tiles_of(name, shape):
            R, C = shape
            out = []
            r = 0
            while r < R:
                nr = min(128, R - r)
                out.append((ap[name][r:r + nr, :], nr, C))
                r += nr
            return out

        for mi, (name, scratch, shape, idx) in enumerate(mats):
            tl = tiles_of(name, shape)
            ncols_total = sum(len(fchunks(ncol, 512)) for _, _, ncol in tl)
            assert ncols_total <= 256
            with nc.Block() as blk:
                qs0 = p.cnt["qs"]; vv0 = p.cnt["vv"]
                @blk.sync
                def _(e, tl=tl, vv0=vv0):
                    for i, (sl, nr, ncol) in enumerate(tl):
                        if i >= 2:
                            e.wait_ge(p.sems["vv"], vv0 + (i - 1))
                        e.dma_start(out=wbuf[i % 2][0:nr, 0:ncol], in_=sl).then_inc(*p.inc("qs", 16))
                @blk.vector
                def _(e, tl=tl, qs0=qs0):
                    e.memset(wide[:], 0.0)
                    e.memset(gapw[:], 0.0)
                    col = 0
                    for i, (sl, nr, ncol) in enumerate(tl):
                        e.wait_ge(p.sems["qs"], qs0 + 16 * (i + 1))
                        ch = fchunks(ncol, 512)
                        ins = None
                        for j, (c0, cw) in enumerate(ch):
                            ins = e.tensor_reduce(wide[0:nr, col:col + 1],
                                                  wbuf[i % 2][0:nr, c0:c0 + cw],
                                                  mybir.AxisListType.X, AL.add,
                                                  apply_absolute_value=True)
                            col += 1
                        ins.then_inc(*p.inc("vv"))
                    e.memset(gapw[:], 0.0)
                    e.memset(gapw[:], 0.0).then_inc(*p.inc("vv"))
                vdone2 = p.cnt["vv"]
                @blk.tensor
                def _(e, vdone2=vdone2, idx=idx, ncols_total=ncols_total):
                    e.wait_ge(p.sems["vv"], vdone2)
                    e.matmul(prow[0:1, 0:ncols_total], ones128[:],
                             wide[:, 0:ncols_total], start=True, stop=True)
                    e.drain()
                    e.nop().then_inc(*p.inc("tv"))
                tdone = p.cnt["tv"]
                @blk.vector
                def _(e, tdone=tdone, idx=idx, shape=shape, ncols_total=ncols_total):
                    e.wait_ge(p.sems["tv"], tdone)
                    ninv = 1.0 / float(shape[0] * shape[1])
                    e.tensor_reduce(red[0:1, 0:1], prow[0:1, 0:ncols_total],
                                    mybir.AxisListType.X, AL.add)
                    e.memset(gapw[:], 0.0)
                    e.memset(gapw[:], 0.0)
                    e.tensor_scalar(mw[0:1, idx:idx + 1], red[0:1, 0:1],
                                    ninv, EPS, AL.mult, AL.max).then_inc(*p.inc("vv"))

        with nc.Block() as blk:
            @blk.vector
            def _(e):
                e.memset(gapt[:], 0.0)
                e.memset(gapt[:], 0.0)
                e.reciprocal(mw_pp[0:1, 0:4], mw[0:1, 0:4]).then_inc(*p.inc("vv"))
            vd = p.cnt["vv"]
            @blk.tensor
            def _(e, vd=vd):
                e.wait_ge(p.sems["vv"], vd)
                e.matmul(prow[0:128, 4:8], ones1x128[:], mw_pp[0:1, 0:4], start=True, stop=True)
                e.drain()
                e.nop().then_inc(*p.inc("tv"))
            td = p.cnt["tv"]
            @blk.vector
            def _(e, td=td):
                e.wait_ge(p.sems["tv"], td)
                e.tensor_scalar(swr[:, 0:4], prow[0:128, 4:8], 1.0, None, AL.mult).then_inc(*p.inc("vv"))

        for mi, (name, scratch, shape, idx) in enumerate(mats):
            tl = tiles_of(name, shape)
            with nc.Block() as blk:
                qs0 = p.cnt["qs"]; qg0 = p.cnt["qg"]; sv0 = p.cnt["sv"]; vv0 = p.cnt["vv"]
                @blk.sync
                def _(e, tl=tl, vv0=vv0):
                    for i, (sl, nr, ncol) in enumerate(tl):
                        if i >= 2:
                            e.wait_ge(p.sems["vv"], vv0 + 2 * (i - 1))
                        e.dma_start(out=wbuf[i % 2][0:nr, 0:ncol], in_=sl).then_inc(*p.inc("qs", 16))
                @blk.scalar
                def _(e, tl=tl, qs0=qs0, vv0=vv0, idx=idx):
                    for i, (sl, nr, ncol) in enumerate(tl):
                        e.wait_ge(p.sems["qs"], qs0 + 16 * (i + 1))
                        if i >= 2:
                            e.wait_ge(p.sems["vv"], vv0 + 2 * (i - 1))
                        e.activation(wbuf[i % 2][0:nr, 0:ncol], wbuf[i % 2][0:nr, 0:ncol],
                                     ACT.Copy, scale=swr[0:nr, idx:idx + 1]).then_inc(*p.inc("sv"))
                @blk.vector
                def _(e, tl=tl, sv0=sv0, qg0=qg0):
                    for i, (sl, nr, ncol) in enumerate(tl):
                        e.wait_ge(p.sems["sv"], sv0 + (i + 1))
                        if i >= 2:
                            e.wait_ge(p.sems["qg"], qg0 + 16 * (i - 1))
                        e.tensor_scalar(wbuf[i % 2][0:nr, 0:ncol], wbuf[i % 2][0:nr, 0:ncol],
                                        1.0, -1.0, AL.min, AL.max).then_inc(*p.inc("vv"))
                        e.tensor_scalar(qbuf[i % 2][0:nr, 0:ncol], wbuf[i % 2][0:nr, 0:ncol],
                                        cmag[0:nr, 0:1], cmag[0:nr, 0:1], AL.add, AL.subtract).then_inc(*p.inc("vv"))
                @blk.gpsimd
                def _(e, tl=tl, vv0=vv0, scratch=scratch):
                    r = 0
                    for i, (sl, nr, ncol) in enumerate(tl):
                        e.wait_ge(p.sems["vv"], vv0 + 2 * (i + 1))
                        e.dma_start(out=scratch[r:r + nr, :], in_=qbuf[i % 2][0:nr, 0:ncol]).then_inc(*p.inc("qg", 16))
                        r += nr
        with nc.Block() as blk:
            qgd = p.cnt["qg"]
            @blk.gpsimd
            def _(e, qgd=qgd):
                e.wait_ge(p.sems["qg"], qgd)
                e.nop().then_inc(*p.inc("gv"))


ENGN = ("sync", "scalar", "vector", "tensor", "gpsimd")

class W:
    def __init__(self, p):
        self.p = p
        self.lists = {e: [] for e in ENGN}

    def wait(self, eng, sem, val):
        s = self.p.sems[sem]
        self.lists[eng].append(lambda e, s=s, v=val: e.wait_ge(s, v))

    def op(self, eng, fn, inc=None, incby=1):
        if inc is None:
            self.lists[eng].append(fn)
            return None
        s, _ = self.p.inc(inc, incby)
        v = self.p.cnt[inc]
        def wrapped(e, fn=fn, s=s, incby=incby):
            ins = fn(e)
            ins.then_inc(s, incby)
        self.lists[eng].append(wrapped)
        return v

    def raw(self, eng, fn):
        self.lists[eng].append(fn)

    def mark(self, eng, inc):
        s, _ = self.p.inc(inc, 1)
        v = self.p.cnt[inc]
        gap = self.p.gap
        if eng == "vector":
            self.lists[eng].append(lambda e, s=s: e.memset(gap[:], 0.0).then_inc(s, 1))
        elif eng == "scalar":
            self.lists[eng].append(lambda e, s=s: e.memzero(gap[:]).then_inc(s, 1))
        else:
            self.lists[eng].append(lambda e, s=s: e.nop().then_inc(s, 1))
        return v

    def drain_mark(self, inc="tv"):
        s, _ = self.p.inc(inc, 1)
        v = self.p.cnt[inc]
        def f(e, s=s):
            e.drain()
            e.nop().then_inc(s, 1)
        self.lists["tensor"].append(f)
        return v

    def replay(self, nc):
        with nc.Block() as blk:
            for eng in ENGN:
                lst = self.lists[eng]
                if not lst:
                    continue
                dec = getattr(blk, eng)
                def body(e, lst=lst):
                    for fn in lst:
                        fn(e)
                dec(body)


def emit_ffn(nc, p, ap, ctx, t, wq_g, wq_u, wq_d, add_scale, debug=()):
    x = t["x"]; cmag = t["cmag"]; meanones = t["meanones"]
    ones128 = t["ones128"]; ones1x128 = t["ones1x128"]; shift = t["shift"]
    rows = t["rows"]; mw = t["mw"]; epscol = t["epscol"]
    pb = t["pb"]; prowA = t["prowA"]; prowB = t["prowB"]
    Tw = t.get("Tw", T)
    its = itile_sizes()

    with ExitStack() as arena:
        sb = lambda name, shape, dt=F32: arena.enter_context(nc.sbuf_tensor(name, shape, dt))
        FW = 343
        xm = [sb(f"ffnxm{g}", [128, FW]) for g in range(NT)]
        q = [sb(f"ffnq{g}", [128, FW], F16) for g in range(NT)]
        ptile = [sb(f"ffnp{i}", [128, FW], F16) for i in range(NI)]
        qp = ptile
        Mx = sb("ffnMx", [128, FW]); Mn = sb("ffnMn", [128, FW]); Mt = sb("ffnMt", [128, FW])
        dsgb = sb("ffndsgb", [128, FW]); dsub = sb("ffndsub", [128, FW]); dsdb = sb("ffndsdb", [128, FW])
        tq = [sb(f"ffntq{i}", [128, FW]) for i in range(2)]
        gf = sb("ffngf", [128, FW]); gs16 = sb("ffngs", [128, FW], F16); uf = sb("ffnuf", [128, FW])
        sq = [sb(f"ffnsq{i}", [128, FW]) for i in range(2)]
        gap = sb("ffngap", [128, 1])
        wpool = sb("ffnwpool", [128, 2, 6528], F16)
        wgb = [wpool[:, i, 0:2048].rearrange("p (t w) -> p t w", w=128) for i in range(2)]
        wub = [wpool[:, i, 2048:4096].rearrange("p (t w) -> p t w", w=128) for i in range(2)]
        wdb = [wpool[:, i, 0:6528].rearrange("p (t w) -> p t w", w=128) for i in range(2)]

        for ft, (f0, fw) in enumerate(fchunks(Tw, FW)):
            w = W(p)
            xs_ = lambda g: x[g][:, f0:f0 + fw]

            vbase = p.cnt["vv"]; tvbase = p.cnt["tv"]; gvb = p.cnt["gv"]; qgb = p.cnt["qg"]
            w.wait("tensor", "vv", vbase)
            qsr, qsv = t.get("x_ready", ("qs", 0))
            w.wait("tensor", qsr, qsv)
            w.wait("scalar", qsr, qsv)
            w.wait("scalar", "vv", vbase)
            w.wait("sync", "gv", gvb)
            w.wait("scalar", "gv", gvb)
            w.wait("sync", "tv", tvbase)
            w.wait("scalar", "tv", tvbase)
            w.wait("vector", "qg", qgb)
            for g in range(NT):
                w.op("tensor", lambda e, g=g, f0=f0, fw=fw: e.matmul(
                    prowA[0:1, 0:fw], meanones[:], xs_(g), start=(g == 0), stop=(g == NT - 1)))
            m_rowA = w.drain_mark()
            sq_consumed = []
            for g in range(NT):
                if g >= 2:
                    w.wait("scalar", "tv", sq_consumed[g - 2])
                sqm_g = w.op("scalar", lambda e, g=g: e.activation(
                    sq[g % 2][:, 0:fw], xs_(g), ACT.Square), inc="sv")
                w.wait("tensor", "sv", sqm_g)
                sq_consumed.append(w.op("tensor", lambda e, g=g: e.matmul(
                    prowB[0:1, 0:fw], meanones[:], sq[g % 2][:, 0:fw],
                    start=(g == 0), stop=(g == NT - 1)), inc="tv"))
            m_rowB = w.drain_mark()
            w.wait("vector", "tv", m_rowB)
            w.op("vector", lambda e: e.tensor_scalar(rows['m'][0:1, 0:fw], prowA[0:1, 0:fw], 1.0, None, AL.mult))
            w.op("vector", lambda e: e.tensor_scalar(rows['msq'][0:1, 0:fw], prowB[0:1, 0:fw], 1.0, None, AL.mult))
            w.op("vector", lambda e: e.memset(gap[:], 0.0))
            w.op("vector", lambda e: e.tensor_tensor(rows['den'][0:1, 0:fw], rows['m'][0:1, 0:fw], rows['m'][0:1, 0:fw], AL.mult))
            w.op("vector", lambda e: e.memset(gap[:], 0.0))
            m_var = w.op("vector", lambda e: e.tensor_tensor(rows['den'][0:1, 0:fw], rows['msq'][0:1, 0:fw], rows['den'][0:1, 0:fw], AL.subtract), inc="vv")
            m_mrow = m_var
            w.wait("scalar", "vv", m_var)
            m_sig = w.op("scalar", lambda e: e.activation(rows['sig'][0:1, 0:fw], rows['den'][0:1, 0:fw], ACT.Sqrt, bias=epscol[0:1, 0:1]), inc="sv")

            pbM = pb[4]
            w.wait("tensor", "vv", m_var)
            w.op("tensor", lambda e: e.matmul(pbM[:, 0:fw], ones1x128[:], rows['m'][0:1, 0:fw], start=True, stop=True))
            m_mb = w.drain_mark()
            w.wait("vector", "tv", m_mb)
            for g in range(NT):
                w.op("vector", lambda e, g=g: e.tensor_tensor(xm[g][:, 0:fw], xs_(g), pbM[:, 0:fw], AL.subtract))
            w.op("vector", lambda e: e.tensor_tensor(Mx[:, 0:fw], xm[0][:, 0:fw], xm[1][:, 0:fw], AL.max))
            w.op("vector", lambda e: e.tensor_tensor(Mn[:, 0:fw], xm[0][:, 0:fw], xm[1][:, 0:fw], AL.min))
            for g in range(2, NT):
                w.op("vector", lambda e, g=g: e.tensor_tensor(Mx[:, 0:fw], Mx[:, 0:fw], xm[g][:, 0:fw], AL.max))
                w.op("vector", lambda e, g=g: e.tensor_tensor(Mn[:, 0:fw], Mn[:, 0:fw], xm[g][:, 0:fw], AL.min))
            m_fold = w.op("vector", lambda e: e.scalar_tensor_tensor(Mt[:, 0:fw], Mn[:, 0:fw], -1.0, Mx[:, 0:fw], AL.mult, AL.max), inc="vv")
            mlast = m_fold
            for k in range(7):
                s = 64 >> k
                w.wait("tensor", "vv", mlast)
                w.op("tensor", lambda e, k=k, s=s: e.matmul(pbM[0:s, 0:fw], shift[k][:], Mt[:, 0:fw], start=True, stop=True))
                tm = w.drain_mark()
                w.wait("vector", "tv", tm)
                mlast = w.op("vector", lambda e, s=s: e.tensor_tensor(Mt[0:s, 0:fw], Mt[0:s, 0:fw], pbM[0:s, 0:fw], AL.max), inc="vv")
            w.wait("vector", "sv", m_sig)
            w.op("vector", lambda e: e.tensor_scalar(rows['se'][0:1, 0:fw], rows['sig'][0:1, 0:fw], EPS, None, AL.mult))
            w.op("vector", lambda e: e.memset(gap[:], 0.0))
            w.op("vector", lambda e: e.tensor_tensor(rows['den'][0:1, 0:fw], Mt[0:1, 0:fw], rows['se'][0:1, 0:fw], AL.max))
            w.op("vector", lambda e: e.memset(gap[:], 0.0))
            w.op("vector", lambda e: e.reciprocal(rows['rcp'][0:1, 0:fw], rows['den'][0:1, 0:fw]))
            w.op("vector", lambda e: e.reciprocal(rows['rs'][0:1, 0:fw], rows['sig'][0:1, 0:fw]))
            w.op("vector", lambda e: e.memset(gap[:], 0.0))
            w.op("vector", lambda e: e.tensor_scalar(rows['q'][0:1, 0:fw], rows['rcp'][0:1, 0:fw], 127.0, None, AL.mult))
            w.op("vector", lambda e: e.tensor_tensor(rows['dd'][0:1, 0:fw], rows['den'][0:1, 0:fw], rows['rs'][0:1, 0:fw], AL.mult))
            w.op("vector", lambda e: e.memset(gap[:], 0.0))
            w.op("vector", lambda e: e.tensor_scalar(rows['g'][0:1, 0:fw], rows['dd'][0:1, 0:fw], mw[0:1, 0:1], 1.0 / 127.0, AL.mult, AL.mult))
            m_rows = w.op("vector", lambda e: e.tensor_scalar(rows['u'][0:1, 0:fw], rows['dd'][0:1, 0:fw], mw[0:1, 1:2], 1.0 / 127.0, AL.mult, AL.mult), inc="vv")
            w.wait("tensor", "vv", m_rows)
            w.op("tensor", lambda e: e.matmul(pbM[:, 0:fw], ones1x128[:], rows['q'][0:1, 0:fw], start=True, stop=True))
            m_qb = w.drain_mark()
            w.wait("vector", "tv", m_qb)
            for g in range(NT):
                w.op("vector", lambda e, g=g: e.tensor_tensor(tq[g % 2][:, 0:fw], xm[g][:, 0:fw], pbM[:, 0:fw], AL.mult))
                w.op("vector", lambda e, g=g: e.tensor_scalar(q[g][:, 0:fw], tq[g % 2][:, 0:fw], cmag[:, 0:1], cmag[:, 0:1], AL.add, AL.subtract))
            m_q = w.mark("vector", "vv")
            for (rw, dst) in (('g', dsgb), ('u', dsub)):
                w.wait("tensor", "vv", m_q if rw == 'g' else m_dsg)
                w.op("tensor", lambda e, rw=rw: e.matmul(pbM[:, 0:fw], ones1x128[:], rows[rw][0:1, 0:fw], start=True, stop=True))
                tm = w.drain_mark()
                w.wait("vector", "tv", tm)
                mm = w.op("vector", lambda e, dst=dst: e.tensor_scalar(dst[:, 0:fw], pbM[:, 0:fw], 1.0, None, AL.mult), inc="vv")
                if rw == 'g':
                    m_dsg = mm
                else:
                    m_dsu = mm

            if True:
                qs0 = p.cnt["qs"]
                evac_mark = {}
                for it, (i0, wi) in enumerate(its):
                    if it >= 2:
                        w.wait("sync", "vv", evac_mark[it - 2])
                    w.op("sync", lambda e, it=it, i0=i0, wi=wi: e.dma_start(
                        out=wgb[it % 2][:, :, 0:wi],
                        in_=wq_g[:, i0:i0 + wi].rearrange("(t p) w -> p t w", p=128)), inc="qs", incby=16)
                    w.op("sync", lambda e, it=it, i0=i0, wi=wi: e.dma_start(
                        out=wub[it % 2][:, :, 0:wi],
                        in_=wq_u[:, i0:i0 + wi].rearrange("(t p) w -> p t w", p=128)), inc="qs", incby=16)
                    if it == 0:
                        w.wait("tensor", "vv", m_q)
                    w.wait("tensor", "qs", qs0 + 32 * (it + 1))
                    if it >= 2:
                        w.wait("tensor", "vv", evac_mark[it - 2])
                    bG, bU = pb[it % 2], pb[2 + it % 2]
                    for g in range(NT):
                        w.op("tensor", lambda e, g=g, it=it, wi=wi, bG=bG: e.matmul(
                            bG[0:wi, 0:fw], wgb[it % 2][:, g, 0:wi], q[g][:, 0:fw],
                            start=(g == 0), stop=(g == NT - 1)))
                    for g in range(NT):
                        w.op("tensor", lambda e, g=g, it=it, wi=wi, bU=bU: e.matmul(
                            bU[0:wi, 0:fw], wub[it % 2][:, g, 0:wi], q[g][:, 0:fw],
                            start=(g == 0), stop=(g == NT - 1)))
                    m_gu = w.drain_mark()
                    w.wait("vector", "tv", m_gu)
                    w.op("vector", lambda e, wi=wi, bG=bG: e.tensor_tensor(gf[0:wi, 0:fw], bG[0:wi, 0:fw], dsgb[0:wi, 0:fw], AL.mult))
                    m_gf = w.mark("vector", "vv")
                    w.wait("scalar", "vv", m_gf)
                    m_silu = w.op("scalar", lambda e, wi=wi: e.activation(gs16[0:wi, 0:fw], gf[0:wi, 0:fw], ACT.Silu), inc="sv")
                    w.op("vector", lambda e, wi=wi, bU=bU: e.tensor_tensor(uf[0:wi, 0:fw], bU[0:wi, 0:fw], dsub[0:wi, 0:fw], AL.mult))
                    w.wait("vector", "sv", m_silu)
                    w.op("vector", lambda e, it=it, wi=wi: e.tensor_tensor(ptile[it][0:wi, 0:fw], gs16[0:wi, 0:fw], uf[0:wi, 0:fw], AL.mult))
                    if it == 0:
                        w.op("vector", lambda e, wi=wi: e.tensor_scalar(Mx[0:wi, 0:fw], ptile[0][0:wi, 0:fw], 1.0, None, AL.mult))
                        ev = w.op("vector", lambda e, wi=wi: e.tensor_scalar(Mn[0:wi, 0:fw], ptile[0][0:wi, 0:fw], 1.0, None, AL.mult), inc="vv")
                    else:
                        w.op("vector", lambda e, it=it, wi=wi: e.tensor_tensor(Mx[0:wi, 0:fw], Mx[0:wi, 0:fw], ptile[it][0:wi, 0:fw], AL.max))
                        ev = w.op("vector", lambda e, it=it, wi=wi: e.tensor_tensor(Mn[0:wi, 0:fw], Mn[0:wi, 0:fw], ptile[it][0:wi, 0:fw], AL.min), inc="vv")
                    evac_mark[it] = ev
            m_fold2 = w.op("vector", lambda e: e.scalar_tensor_tensor(Mt[:, 0:fw], Mn[:, 0:fw], -1.0, Mx[:, 0:fw], AL.mult, AL.max), inc="vv")
            mlast = m_fold2
            for k in range(7):
                s = 64 >> k
                w.wait("tensor", "vv", mlast)
                w.op("tensor", lambda e, k=k, s=s: e.matmul(pbM[0:s, 0:fw], shift[k][:], Mt[:, 0:fw], start=True, stop=True))
                tm = w.drain_mark()
                w.wait("vector", "tv", tm)
                mlast = w.op("vector", lambda e, s=s: e.tensor_tensor(Mt[0:s, 0:fw], Mt[0:s, 0:fw], pbM[0:s, 0:fw], AL.max), inc="vv")
            w.op("vector", lambda e: e.tensor_scalar(rows['den'][0:1, 0:fw], Mt[0:1, 0:fw], EPS, None, AL.max))
            w.op("vector", lambda e: e.memset(gap[:], 0.0))
            w.op("vector", lambda e: e.reciprocal(rows['rcp'][0:1, 0:fw], rows['den'][0:1, 0:fw]))
            w.op("vector", lambda e: e.memset(gap[:], 0.0))
            w.op("vector", lambda e: e.tensor_scalar(rows['q'][0:1, 0:fw], rows['rcp'][0:1, 0:fw], 127.0, None, AL.mult))
            m_rows2 = w.op("vector", lambda e: e.tensor_scalar(rows['d'][0:1, 0:fw], rows['den'][0:1, 0:fw], mw[0:1, 2:3], add_scale / 127.0, AL.mult, AL.mult), inc="vv")
            w.wait("tensor", "vv", m_rows2)
            w.op("tensor", lambda e: e.matmul(pbM[:, 0:fw], ones1x128[:], rows['q'][0:1, 0:fw], start=True, stop=True))
            m_qb2 = w.drain_mark()
            w.wait("vector", "tv", m_qb2)
            for it, (i0, wi) in enumerate(its):
                w.op("vector", lambda e, it=it, wi=wi: e.tensor_tensor(tq[it % 2][0:wi, 0:fw], ptile[it][0:wi, 0:fw], pbM[0:wi, 0:fw], AL.mult))
                w.op("vector", lambda e, it=it, wi=wi: e.tensor_scalar(qp[it][0:wi, 0:fw], tq[it % 2][0:wi, 0:fw], cmag[0:wi, 0:1], cmag[0:wi, 0:1], AL.add, AL.subtract))
            m_qp = w.mark("vector", "vv")
            w.wait("tensor", "vv", m_qp)
            w.op("tensor", lambda e: e.matmul(pbM[:, 0:fw], ones1x128[:], rows['d'][0:1, 0:fw], start=True, stop=True))
            m_db = w.drain_mark()
            w.wait("vector", "tv", m_db)
            m_dsd = w.op("vector", lambda e: e.tensor_scalar(dsdb[:, 0:fw], pbM[:, 0:fw], 1.0, None, AL.mult), inc="vv")

            qa0 = p.cnt["qa"]
            xadd_mark = {}
            for ot in range(NT):
                if ot >= 2:
                    w.wait("scalar", "vv", xadd_mark[ot - 2])
                w.op("scalar", lambda e, ot=ot: e.dma_start(
                    out=wdb[ot % 2][:, 0:50, :],
                    in_=wq_d[0:6400, ot * 128:(ot + 1) * 128].rearrange("(t p) w -> p t w", p=128)), inc="qa", incby=16)
                w.op("scalar", lambda e, ot=ot: e.dma_start(
                    out=wdb[ot % 2][0:64, 50, :],
                    in_=wq_d[6400:6464, ot * 128:(ot + 1) * 128]), inc="qa", incby=16)
                w.wait("tensor", "qa", qa0 + 32 * (ot + 1))
                w.wait("tensor", "vv", m_dsd if ot < 2 else xadd_mark[ot - 2])
                bD = pb[4 + ot % 2] if False else pb[5 - ot % 2]
                for it, (i0, wi) in enumerate(its):
                    w.op("tensor", lambda e, it=it, ot=ot, wi=wi, bD=bD: e.matmul(
                        bD[0:128, 0:fw], wdb[ot % 2][0:wi, it, :], qp[it][0:wi, 0:fw],
                        start=(it == 0), stop=(it == NI - 1)))
                m_dn = w.drain_mark()
                w.wait("vector", "tv", m_dn)
                w.op("vector", lambda e, ot=ot, bD=bD: e.tensor_tensor(tq[ot % 2][:, 0:fw], bD[:, 0:fw], dsdb[:, 0:fw], AL.mult))
                xadd_mark[ot] = w.op("vector", lambda e, ot=ot: e.tensor_tensor(
                    x[ot][:, f0:f0 + fw], x[ot][:, f0:f0 + fw], tq[ot % 2][:, 0:fw], AL.add), inc="vv")
            if "q1" in debug and add_scale == 0.5:
                for g in range(NT):
                    w.wait("gpsimd", "vv", m_q)
                    w.op("gpsimd", lambda e, g=g, f0=f0, fw=fw: e.dma_start(
                        out=ap["dbg_q1"][g * 128:(g + 1) * 128, f0:f0 + fw], in_=q[g][:, 0:fw]), inc="qg", incby=16)
            if "qp" in debug and add_scale == 0.5:
                w.wait("gpsimd", "vv", m_dsd)
                w.op("gpsimd", lambda e, f0=f0, fw=fw: e.dma_start(
                    out=ap["dbg_qp"][:, f0:f0 + fw], in_=qp[0][:, 0:fw]), inc="qg", incby=16)
            if "dsg" in debug and add_scale == 0.5:
                w.wait("gpsimd", "vv", m_dsg)
                w.op("gpsimd", lambda e, f0=f0, fw=fw: e.dma_start(
                    out=ap["dbg_dsg"][:, f0:f0 + fw], in_=dsgb[0:1, 0:fw]), inc="qg", incby=16)
            w.replay(nc)


def emit_ln_norm(nc, p, ap, t, xtiles, out_f16, Tw):
    rows = t["rows"]; meanones = t["meanones"]; ones1x128 = t["ones1x128"]
    epscol = t["epscol"]; prowA = t["prowA"]; prowB = t["prowB"]; pb = t["pb"]
    sq = t["lnsq"]; gap = p.gap
    w = W(p)
    vbase = p.cnt["vv"]
    w.wait("tensor", "vv", vbase)
    w.wait("scalar", "vv", vbase)
    for f0, fw in fchunks(Tw, 350):
        w.wait("tensor", "vv", p.cnt["vv"])
        w.wait("scalar", "vv", p.cnt["vv"])
        sq_consumed = []
        for g in range(NT):
            w.op("tensor", lambda e, g=g, f0=f0, fw=fw: e.matmul(
                prowA[0:1, 0:fw], meanones[:], xtiles[g][:, f0:f0 + fw],
                start=(g == 0), stop=(g == NT - 1)))
        for g in range(NT):
            if g >= 2:
                w.wait("scalar", "tv", sq_consumed[g - 2])
            sm = w.op("scalar", lambda e, g=g, f0=f0, fw=fw: e.activation(
                sq[g % 2][:, 0:fw], xtiles[g][:, f0:f0 + fw], ACT.Square), inc="sv")
            w.wait("tensor", "sv", sm)
            sq_consumed.append(w.op("tensor", lambda e, g=g, fw=fw: e.matmul(
                prowB[0:1, 0:fw], meanones[:], sq[g % 2][:, 0:fw],
                start=(g == 0), stop=(g == NT - 1)), inc="tv"))
        m_rows = w.drain_mark()
        w.wait("vector", "tv", m_rows)
        w.op("vector", lambda e, fw=fw: e.tensor_scalar(rows['m'][0:1, 0:fw], prowA[0:1, 0:fw], 1.0, None, AL.mult))
        w.op("vector", lambda e, fw=fw: e.tensor_scalar(rows['msq'][0:1, 0:fw], prowB[0:1, 0:fw], 1.0, None, AL.mult))
        w.op("vector", lambda e, fw=fw: e.memset(gap[:], 0.0))
        w.op("vector", lambda e, fw=fw: e.tensor_tensor(rows['den'][0:1, 0:fw], rows['m'][0:1, 0:fw], rows['m'][0:1, 0:fw], AL.mult))
        w.op("vector", lambda e, fw=fw: e.memset(gap[:], 0.0))
        m_var = w.op("vector", lambda e, fw=fw: e.tensor_tensor(rows['den'][0:1, 0:fw], rows['msq'][0:1, 0:fw], rows['den'][0:1, 0:fw], AL.subtract), inc="vv")
        w.wait("scalar", "vv", m_var)
        m_sig = w.op("scalar", lambda e, fw=fw: e.activation(rows['sig'][0:1, 0:fw], rows['den'][0:1, 0:fw], ACT.Sqrt, bias=epscol[0:1, 0:1]), inc="sv")
        w.wait("vector", "sv", m_sig)
        m_rs = w.op("vector", lambda e, fw=fw: e.reciprocal(rows['rs'][0:1, 0:fw], rows['sig'][0:1, 0:fw]), inc="vv")
        if "dbg_rows" in ap:
            w.wait("gpsimd", "vv", m_rs)
            w.op("gpsimd", lambda e, f0=f0, fw=fw: e.dma_start(
                out=ap["dbg_rows"][0:1, f0:f0 + fw], in_=rows['m'][0:1, 0:fw]), inc="qg", incby=16)
            w.op("gpsimd", lambda e, f0=f0, fw=fw: e.dma_start(
                out=ap["dbg_rows"][1:2, f0:f0 + fw], in_=rows['sig'][0:1, 0:fw]), inc="qg", incby=16)
        w.wait("tensor", "vv", m_rs)
        w.op("tensor", lambda e, fw=fw: e.matmul(pb[4][:, 0:fw], ones1x128[:], rows['m'][0:1, 0:fw], start=True, stop=True))
        w.op("tensor", lambda e, fw=fw: e.matmul(pb[5][:, 0:fw], ones1x128[:], rows['rs'][0:1, 0:fw], start=True, stop=True))
        m_bc = w.drain_mark()
        w.wait("vector", "tv", m_bc)
        for g in range(NT):
            w.op("vector", lambda e, g=g, f0=f0, fw=fw: e.tensor_tensor(
                sq[g % 2][:, 0:fw], xtiles[g][:, f0:f0 + fw], pb[4][:, 0:fw], AL.subtract))
            w.op("vector", lambda e, g=g, f0=f0, fw=fw: e.tensor_tensor(
                out_f16[g][:, f0:f0 + fw], sq[g % 2][:, 0:fw], pb[5][:, 0:fw], AL.mult))
        m_done = w.mark("vector", "vv")
    w.replay(nc)
    return m_done


def emit_mamba_l1(nc, p, ap, ctx, t, stop_after=None):
    x = t["x"]; pb = t["pb"]; rows = t["rows"]
    ones1x128 = t["ones1x128"]; gap = p.gap
    A_t = t["A_t"]; convb_t = t["convb_t"]; bdt_t = t["bdt_t"]
    with ExitStack() as arena:
        sb = lambda name, shape, dt=F32: arena.enter_context(nc.sbuf_tensor(name, shape, dt))
        xc = [sb(f"mxc{g}", [128, T], F16) for g in range(NT)]

        with ExitStack() as sub:
            sbs = lambda name, shape, dt=F32: sub.enter_context(nc.sbuf_tensor(name, shape, dt))
            wblk = [sbs(f"mwb{i}", [128, 16, 128], F16) for i in range(2)]
            srt = [sbs(f"msr{i}", [128, T], F16) for i in range(2)]
            xln = [sbs(f"mxln{g}", [128, T], F16) for g in range(NT)]
            lnsq = [sbs(f"mlnsq{i}", [128, 512]) for i in range(2)]
            t2 = dict(t); t2["lnsq"] = lnsq
            emit_ln_norm(nc, p, ap, t2, x, xln, T)
            if stop_after == "ln2":
                with nc.Block() as blk:
                    @blk.gpsimd
                    def _(e):
                        e.wait_ge(p.sems["vv"], p.cnt["vv"])
                        for g in range(NT):
                            e.dma_start(out=ap["o_u"][g * 128:(g + 1) * 128, :],
                                        in_=xln[g][:, HALO:T]).then_inc(*p.inc("qg", 16))
                        e.wait_ge(p.sems["qg"], p.cnt["qg"])
                return

            w = W(p)
            w.wait("tensor", "vv", p.cnt["vv"])
            w.wait("tensor", "sv", p.cnt["sv"])
            w.wait("scalar", "vv", p.cnt["vv"])
            w.wait("vector", "sv", p.cnt["sv"])
            evac = {}
            srexp = {}
            grp = 0
            gev = {}
            mm_last = {}
            for ot in range(32):
                if ot >= 2:
                    w.wait("gpsimd", "tv", mm_last[ot - 2])
                ldm = w.op("gpsimd", lambda e, ot=ot: e.dma_start(
                    out=wblk[ot % 2][:], in_=ap["winT"][:, ot * 128:(ot + 1) * 128]
                    .rearrange("(t p) w -> p t w", p=128)), inc="qg", incby=16)
                if ot >= 2:
                    sname, sval = evac[ot - 2]
                    w.wait("tensor", sname, sval)
                if ot >= 2 and (ot - 2) >= 16:
                    w.wait("scalar", "qg", srexp[ot - 2])
                w.wait("tensor", "qg", ldm)
                for seg0, segw in fchunks(T, 512):
                    bk = pb[grp % 4]
                    if grp >= 4:
                        w.wait("tensor", *gev[grp - 4])
                    for g in range(NT):
                        w.op("tensor", lambda e, g=g, ot=ot, seg0=seg0, segw=segw, bk=bk: e.matmul(
                            bk[:, 0:segw], wblk[ot % 2][:, g, :], xln[g][:, seg0:seg0 + segw],
                            start=(g == 0), stop=(g == NT - 1)))
                    mm = w.drain_mark()
                    if ot < 16:
                        w.wait("vector", "tv", mm)
                        ev = w.op("vector", lambda e, ot=ot, seg0=seg0, segw=segw, bk=bk: e.tensor_scalar(
                            xc[ot][:, seg0:seg0 + segw], bk[:, 0:segw], 1.0, None, AL.mult), inc="vv")
                        evac[ot] = ("vv", ev)
                    else:
                        w.wait("scalar", "tv", mm)
                        ev = w.op("scalar", lambda e, ot=ot, seg0=seg0, segw=segw, bk=bk: e.activation(
                            srt[ot % 2][:, seg0:seg0 + segw], bk[:, 0:segw], ACT.Silu), inc="sv")
                        evac[ot] = ("sv", ev)
                    gev[grp] = evac[ot]
                    mm_last[ot] = mm
                    grp += 1
                if ot >= 16:
                    w.wait("gpsimd", "sv", evac[ot][1])
                    srexp[ot] = w.op("gpsimd", lambda e, ot=ot: e.dma_start(
                        out=ap["o_sr"][(ot - 16) * 128:(ot - 15) * 128, :],
                        in_=srt[ot % 2][:, HALO:T]), inc="qg", incby=16)
            w.replay(nc)
        if stop_after == "win":
            return
        if stop_after == "xc":
            with nc.Block() as blk:
                @blk.gpsimd
                def _(e):
                    e.wait_ge(p.sems["vv"], p.cnt["vv"])
                    for g in range(NT):
                        e.dma_start(out=ap["o_u"][g * 128:(g + 1) * 128, :],
                                    in_=xc[g][:, HALO:T]).then_inc(*p.inc("qg", 16))
                    e.wait_ge(p.sems["qg"], p.cnt["qg"])
            return

        wcb_one = sb("mwcb", [128, 64, 128], F16)
        wcb = [wcb_one, wcb_one]
        ut = [sb(f"mu{g}", [128, TC], F16) for g in range(NT)]
        xst = [sb(f"mxs{i}", [128, TC]) for i in range(2)]
        dbc_dt = sb("mdbcdt", [128, TC], F16)
        Bt = sb("mBt", [16, TC], F16)
        Ct = sb("mCt", [16, TC], F16)
        Brep = [sb(f"mbrep{n}", [128, TC], F16) for n in range(NST)]
        wxt = sb("mwx", [128, 16, 160], F16)
        wdtt = sb("mwdt", [128, 2048], F16)
        state = [sb(f"mst{g}", [128, NST]) for g in range(NT)]
        sd = [sb(f"msd{g}", [128, 4]) for g in range(NT)]
        zt = sb("mzt", [128, TC])
        za = sb("mza", [128, TC])
        gapw = sb("mgapw", [128, 128])
        dt_t = [sb(f"mdt{i}", [128, TC]) for i in range(2)]
        du_t = sb("mdu", [128, TC])
        dA_t = [sb(f"mdA{i}", [128, TC]) for i in range(2)]
        dBu_one = sb("mdBu", [128, TC])
        dBu_t = [dBu_one, dBu_one]
        h_t = [sb(f"mh{i}", [128, TC]) for i in range(2)]

        with nc.Block() as blk:
            @blk.gpsimd
            def _(e):
                e.dma_start(out=wxt[:], in_=ap["wxT"].rearrange("(t p) w -> p t w", p=128)).then_inc(*p.inc("qg", 16))
                e.dma_start(out=wdtt[:], in_=ap["wdtT"]).then_inc(*p.inc("qg", 16))
        wx_ld = p.cnt["qg"]

        for c in range(2):
            o0 = HALO + c * TC
            emit_chunk_l1(nc, p, ap, t, dict(
                xc=xc, wcb=wcb, ut=ut, xst=xst, dbc_dt=dbc_dt, Bt=Bt, Ct=Ct,
                Brep=Brep, wxt=wxt, wdtt=wdtt, state=state, sd=sd, zt=zt, za=za,
                dt_t=dt_t, du_t=du_t, dA_t=dA_t, dBu_t=dBu_t, h_t=h_t,
                gapw=gapw, wx_ld=wx_ld), c)
            if stop_after == "chunk0":
                return

        w = W(p)
        vb = p.cnt["vv"]; svb = p.cnt["sv"]
        w.wait("vector", "sv", svb)
        for g in range(NT):
            w.op("vector", lambda e, g=g: e.tensor_tensor(sd[g][:, 2:3], sd[g][:, 0:1], sd[g][:, 1:2], AL.add))
        w.op("vector", lambda e: e.memset(gapw[:], 0.0))
        w.op("vector", lambda e: e.memset(gapw[:], 0.0))
        for g in range(NT):
            w.op("vector", lambda e, g=g: e.tensor_scalar(zt[:, g * NST:(g + 1) * NST], A_t[g][:], sd[g][:, 2:3], None, AL.mult))
        m_ztd = w.mark("vector", "vv")
        w.wait("scalar", "vv", m_ztd)
        for g in range(NT):
            w.op("scalar", lambda e, g=g: e.activation(za[:, g * NST:(g + 1) * NST], zt[:, g * NST:(g + 1) * NST], ACT.Exp))
        m_td = w.mark("scalar", "sv")
        w.wait("gpsimd", "sv", m_td)
        w.wait("gpsimd", "vv", p.cnt["vv"])
        w.op("gpsimd", lambda e: e.dma_start(out=ap["o_Tdec"], in_=za[:, 0:NT * NST]), inc="qg", incby=16)
        for g in range(NT):
            w.op("gpsimd", lambda e, g=g: e.dma_start(
                out=ap["o_F"][:, g * NST:(g + 1) * NST], in_=state[g][:]), inc="qg", incby=16)
        w.replay(nc)


def emit_chunk_l1(nc, p, ap, t, m, c):
    x = t["x"]; pb = t["pb"]
    A_t = t["A_t"]; convb_t = t["convb_t"]; bdt_t = t["bdt_t"]
    xc = m["xc"]; wcb = m["wcb"]; ut = m["ut"]; xst = m["xst"]
    dbc_dt = m["dbc_dt"]; Bt = m["Bt"]; Ct = m["Ct"]; Brep = m["Brep"]
    wxt = m["wxt"]; wdtt = m["wdtt"]; state = m["state"]; sd = m["sd"]
    zt = m["zt"]; za = m["za"]; dt_t = m["dt_t"]; du_t = m["du_t"]
    dA_t = m["dA_t"]; dBu_t = m["dBu_t"]; h_t = m["h_t"]; gapw = m["gapw"]
    bcmask = t["bcmask"]
    o0 = HALO + c * TC
    w = W(p)
    w.wait("tensor", "vv", p.cnt["vv"])
    w.wait("tensor", "sv", p.cnt["sv"])
    w.wait("tensor", "qg", m["wx_ld"])

    evacS = {}; evacV = {}
    conv_mm = {}
    for ot in range(NT):
        ldm = None
        if ot >= 1:
            w.wait("gpsimd", "tv", conv_mm[ot - 1])
        for k in range(4):
            ldm = w.op("gpsimd", lambda e, ot=ot, k=k: e.dma_start(
                out=wcb[ot % 2][:, k * 16:(k + 1) * 16, :],
                in_=ap["cwT"][k][:, ot * 128:(ot + 1) * 128].rearrange("(t p) w -> p t w", p=128)),
                inc="qg", incby=16)
        if ot >= 2:
            w.wait("tensor", "sv", evacS[ot - 2])
            w.wait("tensor", "vv", evacV[ot - 2])
        w.wait("tensor", "qg", ldm)
        bk = pb[ot % 2]
        first = True
        for k in range(4):
            for g in range(NT):
                w.op("tensor", lambda e, g=g, k=k, ot=ot, bk=bk, first=first: e.matmul(
                    bk[:, 0:TC], wcb[ot % 2][:, k * 16 + g, :],
                    xc[g][:, c * TC + k:c * TC + k + TC],
                    start=first, stop=(k == 3 and g == NT - 1)))
                first = False
        mm = w.drain_mark()
        conv_mm[ot] = mm
        if ot >= 2:
            w.wait("vector", "sv", evacS[ot - 2])
        w.wait("vector", "tv", mm)
        xm1 = w.op("vector", lambda e, ot=ot, bk=bk: e.tensor_scalar(
            xst[ot % 2][:], bk[:, 0:TC], convb_t[ot][:, 0:1], None, AL.add), inc="vv")
        vm = w.op("vector", lambda e, ot=ot, o0=o0: e.tensor_tensor(
            x[ot][:, o0:o0 + TC], x[ot][:, o0:o0 + TC], xst[ot % 2][:], AL.add), inc="vv")
        evacV[ot] = vm
        w.wait("scalar", "vv", xm1)
        sm = w.op("scalar", lambda e, ot=ot: e.activation(
            ut[ot][:], xst[ot % 2][:], ACT.Silu), inc="sv")
        evacS[ot] = sm
        w.wait("gpsimd", "sv", sm)
        w.op("gpsimd", lambda e, ot=ot, c=c: e.dma_start(
            out=ap["o_u"][ot * 128:(ot + 1) * 128, c * TC:(c + 1) * TC], in_=ut[ot][:]),
            inc="qg", incby=16)

    w.wait("tensor", "sv", evacS[NT - 1])
    for g in range(NT):
        w.op("tensor", lambda e, g=g: e.matmul(
            pb[2][:, 0:TC], wxt[:, g, 0:128], ut[g][:], start=(g == 0), stop=(g == NT - 1)))
    for g in range(NT):
        w.op("tensor", lambda e, g=g: e.matmul(
            pb[3][0:16, 0:TC], wxt[:, g, 128:144], ut[g][:], start=(g == 0), stop=(g == NT - 1)))
    for g in range(NT):
        w.op("tensor", lambda e, g=g: e.matmul(
            pb[4][0:16, 0:TC], wxt[:, g, 144:160], ut[g][:], start=(g == 0), stop=(g == NT - 1)))
    mwx = w.drain_mark()
    w.wait("vector", "tv", mwx)
    w.op("vector", lambda e: e.tensor_scalar(dbc_dt[:], pb[2][:, 0:TC], 1.0, None, AL.mult))
    w.op("vector", lambda e: e.tensor_scalar(Bt[:], pb[3][0:16, 0:TC], 1.0, None, AL.mult))
    m_bc = w.op("vector", lambda e: e.tensor_scalar(Ct[:], pb[4][0:16, 0:TC], 1.0, None, AL.mult), inc="vv")
    w.wait("gpsimd", "vv", m_bc)
    w.op("gpsimd", lambda e, c=c: e.dma_start(
        out=ap["o_B"][:, c * TC:(c + 1) * TC], in_=Bt[:]), inc="qg", incby=16)
    w.op("gpsimd", lambda e, c=c: e.dma_start(
        out=ap["o_C"][:, c * TC:(c + 1) * TC], in_=Ct[:]), inc="qg", incby=16)

    w.wait("tensor", "vv", m_bc)
    brm = {}
    for n in range(NST):
        bkr = pb[5 - (n % 2)]
        if n >= 2:
            w.wait("tensor", "vv", brm[n - 2])
        w.op("tensor", lambda e, n=n, bkr=bkr: e.matmul(
            bkr[:, 0:TC], bcmask[:, n * 128:(n + 1) * 128], Bt[:], start=True, stop=True))
        tm = w.drain_mark()
        w.wait("vector", "tv", tm)
        brm[n] = w.op("vector", lambda e, n=n, bkr=bkr: e.tensor_scalar(
            Brep[n][:], bkr[:, 0:TC], 1.0, None, AL.mult), inc="vv")

    zt_mark = {}; sdm_d = {}; last_dA = {}; dexp = {}
    for g in range(NT):
        bkd = pb[1 + (g % 2)]
        if g >= 2:
            w.wait("tensor", *zt_mark[g - 2])
        w.op("tensor", lambda e, g=g, bkd=bkd: e.matmul(
            bkd[:, 0:TC], wdtt[:, g * 128:(g + 1) * 128], dbc_dt[:], start=True, stop=True))
        mdl = w.drain_mark()
        w.wait("vector", "tv", mdl)
        ztm = w.op("vector", lambda e, g=g, bkd=bkd: e.tensor_scalar(
            zt[:], bkd[:, 0:TC], bdt_t[g][:, 0:1], None, AL.add), inc="vv")
        zt_mark[g] = ("vv", ztm)
        if g >= 1:
            w.wait("scalar", "vv", sdm_d[g - 1])
        w.wait("scalar", "vv", ztm)
        w.op("scalar", lambda e: e.activation(za[:], zt[:], ACT.Abs))
        w.op("scalar", lambda e: e.activation(za[:], za[:], ACT.Exp, scale=-1.0))
        spm = w.op("scalar", lambda e: e.activation(za[:], za[:], ACT.Ln, bias=1.0), inc="sv")
        if g >= 2:
            w.wait("vector", "sv", last_dA[g - 2])
            w.wait("vector", "qg", dexp[g - 2])
        w.wait("vector", "sv", spm)
        w.op("vector", lambda e, g=g: e.scalar_tensor_tensor(
            dt_t[g % 2][:], zt[:], 0.0, za[:], AL.max, AL.add))
        w.op("vector", lambda e, g=g: e.tensor_tensor(du_t[:], dt_t[g % 2][:], ut[g][:], AL.mult))
        sdm = w.op("vector", lambda e, g=g, c=c: e.tensor_reduce(
            sd[g][:, c:c + 1], dt_t[g % 2][:], mybir.AxisListType.X, AL.add), inc="vv")
        sdm_d[g] = sdm
        w.wait("gpsimd", "vv", sdm)
        dexp[g] = w.op("gpsimd", lambda e, g=g, c=c: e.dma_start(
            out=ap["o_delta"][g * 128:(g + 1) * 128, c * TC:(c + 1) * TC], in_=dt_t[g % 2][:]),
            inc="qg", incby=16)
        scm = {}
        w.wait("scalar", "vv", sdm)
        for n in range(NST):
            if n >= 2:
                w.wait("scalar", "vv", scm[n - 2])
            dam = w.op("scalar", lambda e, g=g, n=n: e.activation(
                dA_t[n % 2][:], dt_t[g % 2][:], ACT.Exp, scale=A_t[g][:, n:n + 1]), inc="sv")
            w.wait("vector", "sv", dam)
            w.op("vector", lambda e, n=n: e.tensor_tensor(
                dBu_t[n % 2][:], du_t[:], Brep[n][:], AL.mult))
            if n >= 1:
                scm[n - 1] = w.op("vector", lambda e, g=g, n=n: e.tensor_scalar(
                    state[g][:, n - 1:n], h_t[(n - 1) % 2][:, TC - 1:TC], 1.0, None, AL.mult), inc="vv")
            init = state[g][:, n:n + 1] if c > 0 else 0.0
            w.op("vector", lambda e, n=n, init=init: e.tensor_tensor_scan(
                h_t[n % 2][:], dA_t[n % 2][:], dBu_t[n % 2][:], init, AL.mult, AL.add))
        w.op("vector", lambda e: e.memset(gapw[:], 0.0))
        w.op("vector", lambda e: e.memset(gapw[:], 0.0))
        scm[NST - 1] = w.op("vector", lambda e, g=g: e.tensor_scalar(
            state[g][:, NST - 1:NST], h_t[(NST - 1) % 2][:, TC - 1:TC], 1.0, None, AL.mult), inc="vv")
        last_dA[g] = dam
    w.replay(nc)


def emit_chunk_l2(nc, p, ap, t, m, c, debug_outv=False):
    pb = t["pb"]; A_t = t["A_t"]; D_t = t["D_t"]; bcmask = t["bcmask"]
    ut = m["ut"]; srt = m["srt"]; Bt = m["Bt"]; Ct = m["Ct"]
    Brep = m["Brep"]; Crep = m["Crep"]; dtb = m["dtb"]; du_t = m["du_t"]
    ys = m["ys"]; yt = m["yt"]; dA_t = m["dA_t"]; dBu = m["dBu"]
    h_t = m["h_t"]; state = m["state"]; init0 = m["init0"]
    outv16 = m["outv16"]; gapw = m["gapw"]
    w = W(p)
    w.wait("tensor", "vv", p.cnt["vv"])
    w.wait("sync", "vv", p.cnt["vv"])

    cs = np.s_[c * TC:(c + 1) * TC]
    bct_m = w.op("sync", lambda e: e.dma_start(out=Bt[:], in_=ap["BT"][:, cs]), inc="qs", incby=16)
    bct_m = w.op("sync", lambda e: e.dma_start(out=Ct[:], in_=ap["CT"][:, cs]), inc="qs", incby=16)
    u_mark = {}; sr_mark = {}
    for g in range(NT):
        u_mark[g] = w.op("sync", lambda e, g=g: e.dma_start(
            out=ut[g][:], in_=ap["uT"][g * 128:(g + 1) * 128, cs]), inc="qs", incby=16)
        sr_mark[g] = w.op("sync", lambda e, g=g: e.dma_start(
            out=srt[g][:], in_=ap["srT"][g * 128:(g + 1) * 128, cs]), inc="qs", incby=16)
    dt_mark = {}
    dt_mark[0] = w.op("sync", lambda e: e.dma_start(
        out=dtb[0][:], in_=ap["deltaT"][0:128, cs]), inc="qs", incby=16)
    dt_mark[1] = w.op("sync", lambda e: e.dma_start(
        out=dtb[1][:], in_=ap["deltaT"][128:256, cs]), inc="qs", incby=16)

    w.wait("tensor", "qs", bct_m)
    brm = {}
    for j in range(2 * NST):
        n = j % NST
        src = Bt if j < NST else Ct
        dst = Brep[n] if j < NST else Crep[n]
        bkr = pb[5 - (j % 2)]
        if j >= 2:
            w.wait("tensor", "vv", brm[j - 2])
        w.op("tensor", lambda e, n=n, bkr=bkr, src=src: e.matmul(
            bkr[:, 0:TC], bcmask[:, n * 128:(n + 1) * 128], src[:], start=True, stop=True))
        tm = w.drain_mark()
        w.wait("vector", "tv", tm)
        brm[j] = w.op("vector", lambda e, bkr=bkr, dst=dst: e.tensor_scalar(
            dst[:], bkr[:, 0:TC], 1.0, None, AL.mult), inc="vv")

    last_dA = {}; du_mark = {}; ov_marks = []
    for g in range(NT):
        if g >= 2:
            w.wait("sync", "sv", last_dA[g - 2])
            w.wait("sync", "vv", du_mark[g - 2])
            dt_mark[g] = w.op("sync", lambda e, g=g: e.dma_start(
                out=dtb[g % 2][:], in_=ap["deltaT"][g * 128:(g + 1) * 128, cs]), inc="qs", incby=16)
        w.wait("vector", "qs", max(dt_mark[g], u_mark[g]))
        du_mark[g] = w.op("vector", lambda e, g=g: e.tensor_tensor(
            du_t[:], dtb[g % 2][:], ut[g][:], AL.mult), inc="vv")
        w.wait("scalar", "vv", du_mark[g])
        w.wait("scalar", "qs", dt_mark[g])
        scm = {}
        for n in range(NST):
            if n >= 2:
                w.wait("scalar", "vv", scm[n - 2])
            dam = w.op("scalar", lambda e, g=g, n=n: e.activation(
                dA_t[n % 2][:], dtb[g % 2][:], ACT.Exp, scale=A_t[g][:, n:n + 1]), inc="sv")
            w.wait("vector", "sv", dam)
            w.op("vector", lambda e, n=n: e.tensor_tensor(
                dBu[:], du_t[:], Brep[n][:], AL.mult))
            if n == 1:
                scm[0] = w.op("vector", lambda e: e.tensor_tensor(
                    ys[:], h_t[0][:], Crep[0][:], AL.mult), inc="vv")
            elif n >= 2:
                w.op("vector", lambda e, n=n: e.tensor_tensor(
                    yt[:], h_t[(n - 1) % 2][:], Crep[n - 1][:], AL.mult))
                scm[n - 1] = w.op("vector", lambda e: e.tensor_tensor(
                    ys[:], ys[:], yt[:], AL.add), inc="vv")
            if c == 0 and n >= 1:
                w.op("vector", lambda e, g=g, n=n: e.tensor_scalar(
                    state[g][:, n - 1:n], h_t[(n - 1) % 2][:, TC - 1:TC], 1.0, None, AL.mult))
            init = init0[:, g * NST + n:g * NST + n + 1] if c == 0 else state[g][:, n:n + 1]
            w.op("vector", lambda e, n=n, init=init: e.tensor_tensor_scan(
                h_t[n % 2][:], dA_t[n % 2][:], dBu[:], init, AL.mult, AL.add))
        last_dA[g] = dam
        w.op("vector", lambda e: e.tensor_tensor(
            yt[:], h_t[(NST - 1) % 2][:], Crep[NST - 1][:], AL.mult))
        w.op("vector", lambda e: e.tensor_tensor(ys[:], ys[:], yt[:], AL.add))
        if c == 0:
            w.op("vector", lambda e, g=g: e.tensor_scalar(
                state[g][:, NST - 1:NST], h_t[(NST - 1) % 2][:, TC - 1:TC], 1.0, None, AL.mult))
        w.op("vector", lambda e, g=g: e.tensor_scalar(
            yt[:], ut[g][:], D_t[g][:, 0:1], None, AL.mult))
        w.op("vector", lambda e: e.tensor_tensor(du_t[:], ys[:], yt[:], AL.add))
        w.wait("vector", "qs", sr_mark[g])
        ovm = w.op("vector", lambda e, g=g: e.tensor_tensor(
            outv16[g][:], du_t[:], srt[g][:], AL.mult), inc="vv")
        ov_marks.append(ovm)
        if debug_outv:
            w.wait("gpsimd", "vv", ovm)
            w.op("gpsimd", lambda e, g=g, c=c: e.dma_start(
                out=ap["dbg_outv"][g * 128:(g + 1) * 128, cs], in_=outv16[g][:]),
                inc="qg", incby=16)
    w.replay(nc)
    return ov_marks[-1]


def emit_wout_chunk(nc, p, ap, t, m, c, outv_done):
    x = t["x"]; pb = t["pb"]
    outv16 = m["outv16"]; wob = m["wob"]
    w = W(p)
    w.wait("tensor", "vv", outv_done)
    w.wait("gpsimd", "tv", p.cnt["tv"])
    ld = {}; mmk = {}; ev = {}
    for ot in range(NT):
        if ot >= 2:
            w.wait("gpsimd", "tv", mmk[ot - 2])
        ld[ot] = w.op("gpsimd", lambda e, ot=ot: e.dma_start(
            out=wob[ot % 2][:], in_=ap["woT"][:, ot * 128:(ot + 1) * 128]
            .rearrange("(t p) w -> p t w", p=128)), inc="qg", incby=16)
        w.wait("tensor", "qg", ld[ot])
        if ot >= 2:
            w.wait("tensor", "vv", ev[ot - 2])
        bk = pb[ot % 2]
        for g in range(NT):
            w.op("tensor", lambda e, g=g, ot=ot, bk=bk: e.matmul(
                bk[:, 0:TC], wob[ot % 2][:, g, :], outv16[g][:],
                start=(g == 0), stop=(g == NT - 1)))
        mmk[ot] = w.drain_mark()
        w.wait("vector", "tv", mmk[ot])
        ev[ot] = w.op("vector", lambda e, ot=ot, c=c, bk=bk: e.tensor_tensor(
            x[ot][:, c * TC:(c + 1) * TC], x[ot][:, c * TC:(c + 1) * TC],
            bk[:, 0:TC], AL.add), inc="vv")
    w.replay(nc)
    return ev[NT - 1]


def build_l2(debug=None, stop_after=None):
    debug = debug or set()
    nc = bass.Bass("TRN2", target_bir_lowering=False, detect_race_conditions=False)
    p = Prog(nc)
    ap = {}
    def din(name, shape, dt=F32):
        ap[name] = nc.dram_tensor(name, shape, dt, kind="ExternalInput").ap()
    def dout(name, shape, dt=F32):
        ap[name] = nc.dram_tensor(name, shape, dt, kind="ExternalOutput").ap()

    din("x1T", [DM, T_OWN])
    din("uT", [DM, T_OWN], F16); din("srT", [DM, T_OWN], F16)
    din("deltaT", [DM, T_OWN]); din("BT", [16, T_OWN], F16); din("CT", [16, T_OWN], F16)
    din("initF", [3, 128, NT * NST]); din("initTd", [3, 128, NT * NST])
    din("woT", [DM, DM])
    din("wqg_in", [DM, IF], F16); din("wqu_in", [DM, IF], F16); din("wqd_in", [IF, DM], F16)
    din("mwin", [1, 4])
    din("A_log", [DM, NST]); din("Dvec", [NT, 128, 1])
    din("c_meanones", [128, 1]); din("c_ones128", [128, 1]); din("c_ones1x128", [1, 128])
    for k in range(7):
        din(f"c_shift{k}", [128, 64 >> k])
    din("c_bcmask", [16, 16 * 128], F16)
    dout("o_out", [DM, T_OWN])
    if stop_after == "scan":
        dout("dbg_outv", [DM, T_OWN], F16)

    with ExitStack() as ctx:
        sb = lambda name, shape, dt=F32: ctx.enter_context(nc.sbuf_tensor(name, shape, dt))
        ps = lambda name, shape: ctx.enter_context(nc.psum_tensor(name, shape, F32))
        for s in ("qs", "qa", "qg", "tv", "vv", "sv", "gv"):
            p.sem(s, ctx)
        p.gap = sb("pgap", [128, 1])

        x = [sb(f"x{g}", [128, T_OWN]) for g in range(NT)]
        cmag = sb("cmag", [128, 1])
        onecol = sb("onecol", [128, 1])
        gap0 = sb("gap0", [128, 1])
        epscol = sb("epscol", [128, 1])
        A_t = [sb(f"A{g}", [128, NST]) for g in range(NT)]
        D_t = [sb(f"Dv{g}", [128, 1]) for g in range(NT)]
        meanones = sb("meanones", [128, 1])
        ones128 = sb("ones128", [128, 1])
        ones1x128 = sb("ones1x128", [1, 128])
        shift = [sb(f"shift{k}", [128, 64 >> k]) for k in range(7)]
        bcmask = sb("bcmask", [16, 16 * 128], F16)
        mw = sb("mw", [1, 4])
        rows = {k: sb(f"row_{k}", [1, ROWW]) for k in
                ("m", "msq", "den", "sig", "se", "rcp", "q", "g", "u", "d", "rs", "dd")}
        init0 = sb("init0", [128, NT * NST])
        iF = [sb(f"iF{j}", [128, NT * NST]) for j in range(3)]
        iT = [sb(f"iT{j}", [128, NT * NST]) for j in range(3)]
        state = [sb(f"st{g}", [128, NST]) for g in range(NT)]

        pb = [ps(f"pb{i}", [128, 512]) for i in range(6)]
        prowA = ps("prowA", [128, 512])
        prowB = ps("prowB", [128, 512])

        with nc.Block() as blk:
            @blk.sync
            def _(e):
                for g in range(NT):
                    e.dma_start(out=x[g][:], in_=ap["x1T"][g * 128:(g + 1) * 128, :]).then_inc(*p.inc("qs", 16))
                e.dma_start(out=meanones[:], in_=ap["c_meanones"]).then_inc(*p.inc("qs", 16))
                e.dma_start(out=ones128[:], in_=ap["c_ones128"]).then_inc(*p.inc("qs", 16))
                e.dma_start(out=ones1x128[:], in_=ap["c_ones1x128"]).then_inc(*p.inc("qs", 16))
                for k in range(7):
                    e.dma_start(out=shift[k][:], in_=ap[f"c_shift{k}"]).then_inc(*p.inc("qs", 16))
                e.dma_start(out=bcmask[:], in_=ap["c_bcmask"]).then_inc(*p.inc("qs", 16))
                e.dma_start(out=mw[:], in_=ap["mwin"]).then_inc(*p.inc("qs", 16))
                for g in range(NT):
                    e.dma_start(out=D_t[g][:], in_=ap["Dvec"][g]).then_inc(*p.inc("qs", 16))
            load_done = p.cnt["qs"]

            @blk.gpsimd
            def _(e):
                for g in range(NT):
                    e.dma_start(out=A_t[g][:], in_=ap["A_log"][g * 128:(g + 1) * 128, :]).then_inc(*p.inc("qg", 16))
                for j in range(3):
                    e.dma_start(out=iF[j][:], in_=ap["initF"][j]).then_inc(*p.inc("qg", 16))
                    e.dma_start(out=iT[j][:], in_=ap["initTd"][j]).then_inc(*p.inc("qg", 16))
            alog_done = p.cnt["qg"]

            @blk.vector
            def _(e):
                e.memset(epscol[:], EPS)
                e.memset(onecol[:], 1.0)
                e.memset(gap0[:], 0.0)
                e.memset(gap0[:], 0.0)
                e.tensor_scalar(cmag[:], onecol[:], T24, T24, AL.add, AL.subtract)
                e.memset(gap0[:], 0.0)
                e.memset(gap0[:], 0.0)
                e.tensor_scalar(cmag[:], cmag[:], D52, C23, AL.mult, AL.add).then_inc(*p.inc("vv"))

            @blk.scalar
            def _(e):
                e.wait_ge(p.sems["qg"], alog_done)
                for g in range(NT):
                    ins = e.activation(A_t[g][:], A_t[g][:], ACT.Exp)
                ins.then_inc(*p.inc("sv"))
            aexp_done = p.cnt["sv"]

            @blk.vector
            def _(e):
                e.wait_ge(p.sems["sv"], aexp_done)
                for g in range(NT):
                    ins = e.tensor_scalar(A_t[g][:], A_t[g][:], -1.0, None, AL.mult)
                ins.then_inc(*p.inc("vv"))
                e.tensor_scalar(init0[:], iF[0][:], 1.0, None, AL.mult)
                e.tensor_tensor(init0[:], init0[:], iT[1][:], AL.mult)
                e.tensor_tensor(init0[:], init0[:], iF[1][:], AL.add)
                e.tensor_tensor(init0[:], init0[:], iT[2][:], AL.mult)
                e.tensor_tensor(init0[:], init0[:], iF[2][:], AL.add).then_inc(*p.inc("vv"))

        with ExitStack() as arena:
            sba = lambda name, shape, dt=F32: arena.enter_context(nc.sbuf_tensor(name, shape, dt))
            ut = [sba(f"su{g}", [128, TC], F16) for g in range(NT)]
            srt = [sba(f"ssr{g}", [128, TC], F16) for g in range(NT)]
            Brep = [sba(f"sB{n}", [128, TC], F16) for n in range(NST)]
            Crep = [sba(f"sC{n}", [128, TC], F16) for n in range(NST)]
            outv16 = [sba(f"sov{g}", [128, TC], F16) for g in range(NT)]
            Bt = sba("sBt", [16, TC], F16)
            Ct = sba("sCt", [16, TC], F16)
            dtb = [sba(f"sdt{i}", [128, TC]) for i in range(2)]
            du_t = sba("sdu", [128, TC])
            ys = sba("sys", [128, TC])
            yt = sba("syt", [128, TC])
            dA_t = [sba(f"sdA{i}", [128, TC]) for i in range(2)]
            dBu = sba("sdBu", [128, TC])
            h_t = [sba(f"sh{i}", [128, TC]) for i in range(2)]
            gapw = sba("sgapw", [128, 128])
            wob = [sba(f"swo{i}", [128, 16, 128], F16) for i in range(2)]
            m = dict(ut=ut, srt=srt, Brep=Brep, Crep=Crep, outv16=outv16, Bt=Bt, Ct=Ct,
                     dtb=dtb, du_t=du_t, ys=ys, yt=yt, dA_t=dA_t, dBu=dBu, h_t=h_t,
                     state=state, init0=init0, gapw=gapw, wob=wob)
            for c in range(2):
                ovd = emit_chunk_l2(nc, p, ap, dict(pb=pb, A_t=A_t, D_t=D_t, bcmask=bcmask, x=x),
                                    m, c, debug_outv=(stop_after == "scan"))
                emit_wout_chunk(nc, p, ap, dict(x=x, pb=pb), m, c, ovd)
            if stop_after == "scan":
                with nc.Block() as blk:
                    @blk.gpsimd
                    def _(e):
                        e.wait_ge(p.sems["qg"], p.cnt["qg"])
                return nc, ap
        x2_done = p.cnt["vv"]
        if stop_after == "wout":
            with nc.Block() as blk:
                @blk.gpsimd
                def _(e):
                    e.wait_ge(p.sems["vv"], x2_done)
                    for g in range(NT):
                        e.dma_start(out=ap["o_out"][g * 128:(g + 1) * 128, :], in_=x[g][:]).then_inc(*p.inc("qg", 16))
                    e.wait_ge(p.sems["qg"], p.cnt["qg"])
            return nc, ap

        ffn_tiles = dict(x=x, cmag=cmag, meanones=meanones, ones128=ones128,
                         ones1x128=ones1x128, shift=shift, rows=rows,
                         mw=mw, mw_pp=None, pb=pb, prowA=prowA, prowB=prowB,
                         epscol=epscol, Tw=T_OWN, x_ready=("vv", x2_done))
        emit_ffn(nc, p, ap, ctx, ffn_tiles, ap["wqg_in"], ap["wqu_in"], ap["wqd_in"],
                 add_scale=0.5)

        with nc.Block() as blk:
            @blk.gpsimd
            def _(e):
                e.wait_ge(p.sems["vv"], p.cnt["vv"])
                e.wait_ge(p.sems["sv"], p.cnt["sv"])
                e.wait_ge(p.sems["tv"], p.cnt["tv"])
                for g in range(NT):
                    e.dma_start(out=ap["o_out"][g * 128:(g + 1) * 128, :], in_=x[g][:]).then_inc(*p.inc("qg", 16))
                e.wait_ge(p.sems["qg"], p.cnt["qg"])
    return nc, ap


def kernel(**inputs):
    consts = host_prep_consts()
    w = weight_prep(inputs)
    xcores = shard_inputs(inputs)

    nc1, _ = build_l1()
    in_maps1 = []
    for c in range(8):
        m = dict(consts)
        m.update({k: np.asarray(v, np.float32) for k, v in w.items()})
        m["xT"] = xcores[c]
        in_maps1.append(m)
    res1 = run_bass_kernel_spmd(nc1, in_maps1, core_ids=list(range(8)))

    nc2, _ = build_l2()
    wq_g = np.asarray(res1.results[0]["wq_g"])
    wq_u = np.asarray(res1.results[0]["wq_u"])
    wq_d = np.asarray(res1.results[0]["wq_d"])
    mwin = np.asarray(res1.results[0]["o_mw"], np.float32)
    in_maps2 = []
    for c in range(8):
        b, chunk = c // 4, c % 4
        o = res1.results[c]
        m = dict(consts)
        m["x1T"] = np.ascontiguousarray(np.asarray(o["o_x1"])[:, HALO:])
        m["uT"] = np.asarray(o["o_u"])
        m["srT"] = np.asarray(o["o_sr"])
        m["deltaT"] = np.asarray(o["o_delta"])
        m["BT"] = np.asarray(o["o_B"])
        m["CT"] = np.asarray(o["o_C"])
        Fs = [np.zeros((128, 256), np.float32)] * (3 - chunk) + \
             [np.asarray(res1.results[b * 4 + j]["o_F"]) for j in range(chunk)]
        Ts = [np.ones((128, 256), np.float32)] * (3 - chunk) + \
             [np.asarray(res1.results[b * 4 + j]["o_Tdec"]) for j in range(chunk)]
        m["initF"] = np.stack(Fs).astype(np.float32)
        m["initTd"] = np.stack(Ts).astype(np.float32)
        m["woT"] = np.asarray(w["woT"], np.float32)
        m["wqg_in"] = wq_g; m["wqu_in"] = wq_u; m["wqd_in"] = wq_d
        m["mwin"] = mwin
        m["A_log"] = np.asarray(w["A_log"], np.float32)
        m["Dvec"] = np.asarray(w["Dvec"], np.float32)
        in_maps2.append(m)
    res2 = run_bass_kernel_spmd(nc2, in_maps2, core_ids=list(range(8)))

    out = np.empty((2, 4096, 2048), np.float32)
    for c in range(8):
        b, chunk = c // 4, c % 4
        out[b, chunk * T_OWN:(chunk + 1) * T_OWN, :] = \
            np.asarray(res2.results[c]["o_out"], np.float32).T
    return out
